# version 01
# baseline (speedup 1.0000x reference)
import numpy as np
import concourse.bass as bass
import concourse.mybir as mybir
from concourse.bass_utils import run_bass_kernel_spmd

F32 = mybir.dt.float32
BF16 = mybir.dt.bfloat16
AF = mybir.ActivationFunctionType
ALU = mybir.AluOpType
AX = mybir.AxisListType

D = 1024
T = 4096
FH = 4096
QT = 1024
NQ = 4
NFT = D // 128
NHT = FH // 128


class Sems:

    def __init__(self, nc, names):
        self.nc = nc
        self.h = {}
        self.c = {}
        self._ctx = []
        for n in names:
            cm = nc.semaphore(n)
            h = cm.__enter__()
            self._ctx.append(cm)
            self.h[n] = h
            self.c[n] = 0

    def inc(self, instr, name, n=1):
        instr.then_inc(self.h[name], n)
        self.c[name] += n
        return self.c[name]

    def wait(self, eng, name, value=None):
        eng.wait_ge(self.h[name], self.c[name] if value is None else value)

    def val(self, name):
        return self.c[name]


class Chain:

    def __init__(self, s, eng, sem):
        self.s, self.eng, self.sem = s, eng, sem

    def op(self, fn, *a, **k):
        if self.s.c[self.sem] > 0:
            self.s.wait(self.eng, self.sem)
        instr = fn(*a, **k)
        self.s.inc(instr, self.sem)
        return self.s.val(self.sem)


def emit_ffn(nc, s: Sems, bufs, x_in, x_out, W, scl_off):
    xbf = bufs["xbf"]
    xpr = bufs["xpr"]
    xn = bufs["xn"]
    hb = bufs["hb"]
    gA = bufs["gA"]
    gB = bufs["gB"]
    wa = bufs["wa"]
    wb2 = bufs["wb2"]
    wg = bufs["wg"]
    x2f = bufs["x2f"]
    sil = bufs["sil"]
    rows = bufs["rows"]
    scl = bufs["scl"]
    ones128 = bufs["ones128"]
    ones1 = bufs["ones1"]
    cst = bufs["cst"]
    ps = bufs["ps"]
    xnew = gA[:].bitcast(F32)

    s1c = lambda of: scl[:, scl_off + of:scl_off + of + 1]
    s2c = lambda of: scl[:, scl_off + 32 + of:scl_off + 32 + of + 1]
    sh1c = lambda of: scl[:, scl_off + 64 + of:scl_off + 64 + of + 1]
    sh2c = lambda of: scl[:, scl_off + 96 + of:scl_off + 96 + of + 1]
    s3hc = lambda of: scl[:, scl_off + 128 + of:scl_off + 128 + of + 1]

    gp, sy, te, sc, ve = nc.gpsimd, nc.sync, nc.tensor, nc.scalar, nc.vector
    V = Chain(s, ve, "sve")
    S = Chain(s, sc, "ssc")

    for q in range(NQ):
        tok0 = q * QT
        for tt in range(2):
            c0 = tok0 + tt * 512
            if q > 0 or tt > 0:
                s.wait(gp, "sve", s.val("sve"))
            sx_here = s.inc(gp.dma_start(
                xbf[:].rearrange("p (i c) -> p i c", i=8),
                x_in[:, c0:c0 + 512].rearrange("(i p) c -> p i c", p=128)),
                "sx", 16)
            m_row = rows[0:1, 0:512]
            tmpa = rows[0:1, 512:1024]
            tmpb = rows[0:1, 1024:1536]
            s_row = rows[0:1, 1536:2048]
            b_row = rows[0:1, 2048:2560]
            s.wait(sc, "sx", sx_here)
            s.wait(te, "sx", sx_here)
            sq_idx = []
            for ft in range(NFT):
                slot = ft % 2
                if ft >= 2:
                    s.wait(sc, "smm", s.val("smm"))
                sq_idx.append(S.op(sc.activation,
                                   x2f[:, slot * 512:slot * 512 + 512],
                                   xbf[:, ft * 512:(ft + 1) * 512], AF.Square))
                te.matmul(ps[0:1, 4 * 512:4 * 512 + 512],
                          ones128[:], xbf[:, ft * 512:(ft + 1) * 512],
                          start=(ft == 0), stop=(ft == NFT - 1))
                s.wait(te, "ssc", sq_idx[ft])
                mm2 = te.matmul(ps[0:1, 5 * 512:5 * 512 + 512],
                                ones128[:], x2f[:, slot * 512:slot * 512 + 512],
                                start=(ft == 0), stop=(ft == NFT - 1))
                s.inc(mm2, "smm")
            stats_done = s.val("smm")
            s.wait(ve, "smm", stats_done)
            V.op(ve.tensor_scalar, m_row, ps[0:1, 4 * 512:4 * 512 + 512],
                 1.0 / D, None, ALU.mult)
            V.op(ve.tensor_tensor, tmpa, m_row, m_row, ALU.mult)
            var_idx = V.op(ve.scalar_tensor_tensor, tmpb,
                           ps[0:1, 5 * 512:5 * 512 + 512], 1.0 / D, tmpa,
                           ALU.mult, ALU.subtract)
            s.wait(sc, "sve", var_idx)
            S.op(sc.activation, tmpa, tmpb, AF.Ln, bias=cst[0:1, 0:1])
            srow_idx = S.op(sc.activation, s_row, tmpa, AF.Exp, scale=-0.5)
            s.wait(ve, "ssc", srow_idx)
            brow_idx = V.op(ve.scalar_tensor_tensor, b_row, m_row, -1.0, s_row,
                            ALU.mult, ALU.mult)
            s.wait(te, "sve", brow_idx)
            te.matmul(ps[:, 6 * 512:6 * 512 + 512], ones1[:], s_row,
                      start=True, stop=True)
            s.inc(te.matmul(ps[:, 7 * 512:7 * 512 + 512], ones1[:], b_row,
                            start=True, stop=True), "smm")
            s.wait(ve, "smm", s.val("smm"))
            for ft in range(NFT):
                dsl = slice(ft * QT + tt * 512, ft * QT + tt * 512 + 512)
                V.op(ve.tensor_tensor, sil[:, 0:512],
                     xbf[:, ft * 512:(ft + 1) * 512],
                     ps[:, 6 * 512:6 * 512 + 512], ALU.mult)
                V.op(ve.tensor_tensor, xn[:, dsl], sil[:, 0:512],
                     ps[:, 7 * 512:7 * 512 + 512], ALU.add)
        sve_xn = s.val("sve")

        mm_of = []
        ve_of = []
        for of in range(NHT):
            par = of % 2
            if of >= 2:
                s.wait(gp, "smm", mm_of[of - 2])
            elif q > 0:
                s.wait(gp, "smm", last_par_mm[par])
            wsem = "swA" if par == 0 else "swB"
            s.inc(gp.dma_start(
                wa[:, par * 1024:(par + 1) * 1024].rearrange("p (i c) -> p i c", i=8),
                W["w1T"][:, of * 128:(of + 1) * 128].rearrange("(i p) c -> p i c", p=128)),
                wsem, 16)
            s.inc(gp.dma_start(
                wb2[:, par * 1024:(par + 1) * 1024].rearrange("p (i c) -> p i c", i=8),
                W["w2T"][:, of * 128:(of + 1) * 128].rearrange("(i p) c -> p i c", p=128)),
                wsem, 16)
            s.wait(te, wsem, s.val(wsem))
            if of == 0:
                s.wait(te, "sve", sve_xn)
            if of >= 2:
                s.wait(te, "sve", ve_of[of - 2])
            ba1 = par * 2
            ba2 = 4 + par * 2
            for i in range(NFT):
                lt = wa[:, par * 1024 + i * 128:par * 1024 + (i + 1) * 128]
                for tt in range(2):
                    te.matmul(ps[:, (ba1 + tt) * 512:(ba1 + tt) * 512 + 512], lt,
                              xn[:, i * QT + tt * 512:i * QT + tt * 512 + 512],
                              start=(i == 0), stop=(i == NFT - 1))
            for i in range(NFT):
                lt = wb2[:, par * 1024 + i * 128:par * 1024 + (i + 1) * 128]
                for tt in range(2):
                    mm = te.matmul(ps[:, (ba2 + tt) * 512:(ba2 + tt) * 512 + 512], lt,
                                   xn[:, i * QT + tt * 512:i * QT + tt * 512 + 512],
                                   start=(i == 0), stop=(i == NFT - 1))
            mm_of.append(s.inc(mm, "smm"))
            s.wait(sc, "smm", mm_of[of])
            if of > 0:
                s.wait(sc, "sve", ve_of[of - 1])
            sidx = []
            for tt in range(2):
                sidx.append(S.op(sc.activation, sil[:, tt * 512:(tt + 1) * 512],
                                 ps[:, (ba1 + tt) * 512:(ba1 + tt) * 512 + 512],
                                 AF.Silu, scale=s1c(of)))
            s.wait(ve, "ssc", sidx[-1])
            for tt in range(2):
                vv = V.op(ve.scalar_tensor_tensor,
                          hb[:, of * QT + tt * 512:of * QT + tt * 512 + 512],
                          ps[:, (ba2 + tt) * 512:(ba2 + tt) * 512 + 512], s2c(of),
                          sil[:, tt * 512:(tt + 1) * 512], ALU.mult, ALU.mult)
            ve_of.append(vv)
        last_par_mm = {0: mm_of[NHT - 2], 1: mm_of[NHT - 1]}
        sve_h = s.val("sve")

        for stage in range(2):
            wkey = "h1T" if stage == 0 else "h2T"
            shc = sh1c if stage == 0 else sh2c
            sc_of = []
            mm_of2 = []
            for of in range(NHT):
                par = of % 2
                if of >= 2:
                    s.wait(gp, "smm", mm_of2[of - 2])
                else:
                    s.wait(gp, "smm", last_par_mm[par])
                wsem = "swA" if par == 0 else "swB"
                s.inc(gp.dma_start(
                    wg[:, par * 4096:(par + 1) * 4096].rearrange("p (i c) -> p i c", i=32),
                    W[wkey][:, of * 128:(of + 1) * 128].rearrange("(i p) c -> p i c", p=128)),
                    wsem, 16)
                s.wait(te, wsem, s.val(wsem))
                if of == 0:
                    if stage == 0:
                        s.wait(te, "sve", sve_h)
                    else:
                        s.wait(te, "ssc", sc_prev)
                if of >= 2:
                    s.wait(te, "ssc", sc_of[of - 2])
                for tt in range(2):
                    bank = par * 2 + tt
                    for i in range(NHT):
                        if stage == 0:
                            rh = hb[:, i * QT + tt * 512:i * QT + tt * 512 + 512]
                        else:
                            gsrc = gA if i < 16 else gB
                            ii = i % 16
                            rh = gsrc[:, ii * QT + tt * 512:ii * QT + tt * 512 + 512]
                        mm = te.matmul(ps[:, bank * 512:bank * 512 + 512],
                                       wg[:, par * 4096 + i * 128:par * 4096 + (i + 1) * 128],
                                       rh, start=(i == 0), stop=(i == NHT - 1))
                mm_of2.append(s.inc(mm, "smm"))
                s.wait(sc, "smm", mm_of2[of])
                for tt in range(2):
                    bank = par * 2 + tt
                    if stage == 0:
                        gdst = gA if of < 16 else gB
                        oo = of % 16
                        dst = gdst[:, oo * QT + tt * 512:oo * QT + tt * 512 + 512]
                    else:
                        dst = hb[:, of * QT + tt * 512:of * QT + tt * 512 + 512]
                    ssx = S.op(sc.activation, dst,
                               ps[:, bank * 512:bank * 512 + 512],
                               AF.Gelu, scale=shc(of))
                sc_of.append(ssx)
            sc_prev = s.val("ssc")
            mm_g_last = mm_of2
            last_par_mm = {0: mm_of2[NHT - 2], 1: mm_of2[NHT - 1]}

        ve_of3 = []
        mm_p5 = []
        for of in range(NFT):
            par = of % 2
            if of >= 2:
                s.wait(sy, "sve", ve_of3[of - 2])
            s.inc(sy.dma_start(xpr[:, par * QT:(par + 1) * QT],
                               x_in[of * 128:(of + 1) * 128, tok0:tok0 + QT]),
                  "sxr", 16)
            if of < 2:
                s.wait(gp, "smm", last_par_mm[par])
            else:
                s.wait(gp, "smm", s.val("smm"))
            wsem = "swA" if par == 0 else "swB"
            s.inc(gp.dma_start(
                wg[:, par * 4096:par * 4096 + 4096].rearrange("p (i c) -> p i c", i=32),
                W["w3T"][:, of * 128:(of + 1) * 128].rearrange("(i p) c -> p i c", p=128)),
                wsem, 16)
            s.wait(te, wsem, s.val(wsem))
            if of == 0:
                s.wait(te, "ssc", sc_prev)
            if of >= 2:
                s.wait(te, "sve", ve_of3[of - 2])
            for tt in range(2):
                bank = par * 2 + tt
                for i in range(NHT):
                    mm = te.matmul(ps[:, bank * 512:bank * 512 + 512],
                                   wg[:, par * 4096 + i * 128:par * 4096 + (i + 1) * 128],
                                   hb[:, i * QT + tt * 512:i * QT + tt * 512 + 512],
                                   start=(i == 0), stop=(i == NHT - 1))
            s.inc(mm, "smm")
            s.wait(ve, "smm", s.val("smm"))
            s.wait(ve, "sxr", s.val("sxr"))
            for tt in range(2):
                bank = par * 2 + tt
                vv = V.op(ve.scalar_tensor_tensor,
                          xnew[:, of * QT + tt * 512:of * QT + tt * 512 + 512],
                          ps[:, bank * 512:bank * 512 + 512], s3hc(of),
                          xpr[:, par * QT + tt * 512:par * QT + tt * 512 + 512],
                          ALU.mult, ALU.add)
            ve_of3.append(vv)
            mm_p5.append(s.val("smm"))
            s.wait(sy, "sve", ve_of3[of])
            s.inc(sy.dma_start(x_out[of * 128:(of + 1) * 128, tok0:tok0 + QT],
                               xnew[:, of * QT:(of + 1) * QT]), "sxo", 16)
        last_par_mm = {0: mm_p5[NFT - 2], 1: mm_p5[NFT - 1]}
        s.wait(sc, "sxo", s.val("sxo"))




def alloc_l0attn_bufs(nc, ctx_list):
    def t(name, shape, dt):
        cm = nc.sbuf_tensor(name, shape, dt)
        h = cm.__enter__()
        ctx_list.append(cm)
        return h

    bufs = dict(
        krot=t("krot", [128, 4 * 4096], BF16),
        kpl=t("kpl", [128, 4 * 4096], BF16),
        vtm=t("vtm", [128, 32 * 512], BF16),
        yg=t("yg", [128, 4 * 4096], BF16),
        qpl=t("qpl", [128, 4 * 512], BF16),
        qrot=t("qrot", [128, 4 * 512], BF16),
        xs=t("xs", [128, 2 * 1024], BF16),
        wq=t("wq", [128, 8 * 512], BF16),
        wk=t("wk", [128, 8 * 512], BF16),
        wv=t("wv", [128, 8 * 512], BF16),
        pscr=t("pscr", [128, 512], F32),
        rscr=t("rscr", [128, 512], F32),
        ttmp=t("ttmp", [128, 1024], BF16),
        costm=t("costm", [128, 1024], F32),
        sintm=t("sintm", [128, 1024], F32),
        idb=t("idb", [128, 128], BF16),
        idf=t("idf", [128, 128], F32),
        masks=t("masks", [128, 4 * 512], BF16),
        den=t("den", [128, 32], F32),
        rec=t("rec", [128, 32], F32),
        recrep=t("recrep", [128, 512], F32),
        e1=t("e1", [128, 2 * 1024], BF16),
        woc=t("woc", [128, 2 * 512], BF16),
        boutc=t("boutc", [128, 8], F32),
        ones128=t("ones128", [128, 1], BF16),
        ones1=t("ones1", [1, 128], F32),
        cst=t("cst", [128, 8], F32),
    )
    cm = nc.psum_tensor("ps", [128, 6 * 512], F32)
    bufs["ps"] = cm.__enter__()
    ctx_list.append(cm)
    cm = nc.psum_tensor("psb", [128, 2 * 1024], BF16)
    bufs["psb"] = cm.__enter__()
    ctx_list.append(cm)
    return bufs


def emit_ln_to_dram(nc, s, bufs, x_in, xn_d):
    gp, sy, te, sc, ve = nc.gpsimd, nc.sync, nc.tensor, nc.scalar, nc.vector
    V = Chain(s, ve, "sve")
    S = Chain(s, sc, "ssc")
    yg = bufs["yg"]
    xbf = yg[:, 0:4096]
    xnout = yg[:, 4096:8192]
    x2f = yg[:, 8192:9216]
    rows = yg[0:1, 9216:14336].bitcast(F32)
    ones128, ones1, cst, ps = bufs["ones128"], bufs["ones1"], bufs["cst"], bufs["ps"]
    for tt in range(8):
        c0 = tt * 512
        if tt > 0:
            s.wait(gp, "sve", s.val("sve"))
        sx_here = s.inc(gp.dma_start(
            xbf.rearrange("p (i c) -> p i c", i=8),
            x_in[:, c0:c0 + 512].rearrange("(i p) c -> p i c", p=128)), "sx", 16)
        m_row = rows[0:1, 0:512]
        tmpa = rows[0:1, 512:1024]
        tmpb = rows[0:1, 1024:1536]
        s_row = rows[0:1, 1536:2048]
        b_row = rows[0:1, 2048:2560]
        s.wait(sc, "sx", sx_here)
        s.wait(te, "sx", sx_here)
        for ft in range(NFT):
            slot = ft % 2
            if ft >= 2:
                s.wait(sc, "smm", s.val("smm"))
            sqi = S.op(sc.activation, x2f[:, slot * 512:slot * 512 + 512],
                       xbf[:, ft * 512:(ft + 1) * 512], AF.Square)
            te.matmul(ps[0:1, 4 * 512:4 * 512 + 512], ones128[:],
                      xbf[:, ft * 512:(ft + 1) * 512],
                      start=(ft == 0), stop=(ft == NFT - 1))
            s.wait(te, "ssc", sqi)
            s.inc(te.matmul(ps[0:1, 5 * 512:5 * 512 + 512], ones128[:],
                            x2f[:, slot * 512:slot * 512 + 512],
                            start=(ft == 0), stop=(ft == NFT - 1)), "smm")
        s.wait(ve, "smm", s.val("smm"))
        V.op(ve.tensor_scalar, m_row, ps[0:1, 4 * 512:4 * 512 + 512], 1.0 / D, None, ALU.mult)
        V.op(ve.tensor_tensor, tmpa, m_row, m_row, ALU.mult)
        vi = V.op(ve.scalar_tensor_tensor, tmpb, ps[0:1, 5 * 512:5 * 512 + 512],
                  1.0 / D, tmpa, ALU.mult, ALU.subtract)
        s.wait(sc, "sve", vi)
        S.op(sc.activation, tmpa, tmpb, AF.Ln, bias=cst[0:1, 0:1])
        si = S.op(sc.activation, s_row, tmpa, AF.Exp, scale=-0.5)
        s.wait(ve, "ssc", si)
        bi = V.op(ve.scalar_tensor_tensor, b_row, m_row, -1.0, s_row, ALU.mult, ALU.mult)
        s.wait(te, "sve", bi)
        te.matmul(ps[:, 2 * 512:2 * 512 + 512], ones1[:], s_row, start=True, stop=True)
        s.inc(te.matmul(ps[:, 3 * 512:3 * 512 + 512], ones1[:], b_row,
                        start=True, stop=True), "smm")
        s.wait(ve, "smm", s.val("smm"))
        if tt > 0:
            s.wait(ve, "sy", s.val("sy"))
        for ft in range(NFT):
            V.op(ve.tensor_tensor, bufs["pscr"][:, 0:512],
                 xbf[:, ft * 512:(ft + 1) * 512], ps[:, 2 * 512:2 * 512 + 512], ALU.mult)
            V.op(ve.tensor_tensor, xnout[:, ft * 512:(ft + 1) * 512],
                 bufs["pscr"][:, 0:512], ps[:, 3 * 512:3 * 512 + 512], ALU.add)
        s.wait(sy, "sve", s.val("sve"))
        s.inc(sy.dma_start(
            xn_d[:, c0:c0 + 512].rearrange("(i p) c -> p i c", p=128),
            xnout.rearrange("p (i c) -> p i c", i=8)), "sy", 16)
    s.wait(gp, "sy", s.val("sy"))


def _rot_ops(V, ve, pscr, out_tm, costb, sintb, rscr):
    te_v = bass.AP(pscr.tensor, pscr.offset, [[512, 128], [64, 8], [2, 32]])
    to_v = bass.AP(pscr.tensor, pscr.offset + 1, [[512, 128], [64, 8], [2, 32]])
    re_v = bass.AP(out_tm.tensor, out_tm.offset, [[out_tm.tensor.shape[1], 128], [64, 8], [1, 32]])
    ro_v = bass.AP(out_tm.tensor, out_tm.offset + 32, [[out_tm.tensor.shape[1], 128], [64, 8], [1, 32]])
    t1 = rscr[:, 0:256].rearrange("p (h i) -> p h i", h=8)
    t2 = rscr[:, 256:512].rearrange("p (h i) -> p h i", h=8)
    V.op(ve.tensor_tensor, t1, te_v, costb, ALU.mult)
    V.op(ve.tensor_tensor, t2, to_v, sintb, ALU.mult)
    V.op(ve.tensor_tensor, re_v, t1, t2, ALU.subtract)
    V.op(ve.tensor_tensor, t1, te_v, sintb, ALU.mult)
    V.op(ve.tensor_tensor, t2, to_v, costb, ALU.mult)
    return V.op(ve.tensor_tensor, ro_v, t1, t2, ALU.add)


def emit_l0attn(nc, s, bufs, x_in, x_out, xn_d, wqkv, wout):
    gp, sy, te, sc, ve = nc.gpsimd, nc.sync, nc.tensor, nc.scalar, nc.vector
    V = Chain(s, ve, "sve")
    S = Chain(s, sc, "ssc")
    b = bufs
    ps, psb = b["ps"], b["psb"]

    emit_ln_to_dram(nc, s, bufs, x_in, xn_d)

    for g in range(2):
        if g == 1:
            s.wait(gp, "smm", s.val("smm"))
        for wbuf, base in ((b["wq"], 0), (b["wk"], 1024), (b["wv"], 2048)):
            s.inc(gp.dma_start(
                wbuf[:].rearrange("p (i c) -> p i c", i=8),
                wqkv[:, base + g * 512: base + (g + 1) * 512]
                .rearrange("(i p) c -> p i c", p=128)), "sx", 16)
        sx_w = s.val("sx")
        s.wait(te, "sx", sx_w)

        ve_tt = []
        mm_tt = []
        for tt in range(32):
            par = tt % 2
            if tt >= 2:
                s.wait(sy, "smm", mm_tt[tt - 2])
            else:
                s.wait(sy, "sy", s.val("sy"))
            xsem = "syA" if par == 0 else "syB"
            s.inc(sy.dma_start(
                b["xs"][:, par * 1024:(par + 1) * 1024].rearrange("p (i c) -> p i c", i=8),
                xn_d[:, tt * 128:(tt + 1) * 128].rearrange("(i p) c -> p i c", p=128)),
                xsem, 16)
            s.wait(te, xsem, s.val(xsem))
            if tt >= 2:
                s.wait(te, "sve", ve_tt[tt - 2])
            bk, bv = par * 2, par * 2 + 1
            for i in range(8):
                lt = b["xs"][:, par * 1024 + i * 128:par * 1024 + (i + 1) * 128]
                te.matmul(ps[:, bk * 512:bk * 512 + 512], lt, b["wk"][:, i * 512:(i + 1) * 512],
                          start=(i == 0), stop=(i == 7))
            for i in range(8):
                lt = b["xs"][:, par * 1024 + i * 128:par * 1024 + (i + 1) * 128]
                mm = te.matmul(ps[:, bv * 512:bv * 512 + 512], lt, b["wv"][:, i * 512:(i + 1) * 512],
                               start=(i == 0), stop=(i == 7))
            mm_tt.append(s.inc(mm, "smm"))
            s.wait(ve, "smm", mm_tt[-1])
            V.op(ve.tensor_copy, b["vtm"][:, tt * 512:(tt + 1) * 512], ps[:, bv * 512:bv * 512 + 512])
            V.op(ve.tensor_copy, b["pscr"][:], ps[:, bk * 512:bk * 512 + 512])
            V.op(ve.tensor_copy, b["ttmp"][:, 512:1024], ps[:, bk * 512:bk * 512 + 512])
            costb = b["costm"][:, tt * 32:(tt + 1) * 32].unsqueeze(1).to_broadcast([128, 8, 32])
            sintb = b["sintm"][:, tt * 32:(tt + 1) * 32].unsqueeze(1).to_broadcast([128, 8, 32])
            rot_i = _rot_ops(V, ve, b["pscr"][:], b["ttmp"][:, 0:512], costb, sintb, b["rscr"])
            s.wait(te, "sve", rot_i)
            for j in range(4):
                te.transpose(psb[:, par * 1024 + j * 128:par * 1024 + (j + 1) * 128],
                             b["ttmp"][:, j * 128:(j + 1) * 128], b["idb"][:])
            tmm = None
            for j in range(4):
                tmm = te.transpose(psb[:, par * 1024 + 512 + j * 128:par * 1024 + 512 + (j + 1) * 128],
                                   b["ttmp"][:, 512 + j * 128:512 + (j + 1) * 128], b["idb"][:])
            s.inc(tmm, "smm")
            s.wait(ve, "smm", s.val("smm"))
            for j in range(4):
                V.op(ve.tensor_copy, b["krot"][:, j * 4096 + tt * 128:j * 4096 + (tt + 1) * 128],
                     psb[:, par * 1024 + j * 128:par * 1024 + (j + 1) * 128])
            vv = None
            for j in range(4):
                vv = V.op(ve.tensor_copy, b["kpl"][:, j * 4096 + tt * 128:j * 4096 + (tt + 1) * 128],
                          psb[:, par * 1024 + 512 + j * 128:par * 1024 + 512 + (j + 1) * 128])
            ve_tt.append(vv)

        for qt in range(8):
            for sub in range(4):
                tt = qt * 4 + sub
                par = sub % 2
                s.wait(sy, "smm", s.val("smm"))
                xsem = "syA" if par == 0 else "syB"
                s.inc(sy.dma_start(
                    b["xs"][:, par * 1024:(par + 1) * 1024].rearrange("p (i c) -> p i c", i=8),
                    xn_d[:, tt * 128:(tt + 1) * 128].rearrange("(i p) c -> p i c", p=128)),
                    xsem, 16)
                s.wait(te, xsem, s.val(xsem))
                s.wait(te, "sve", s.val("sve"))
                bq = par
                for i in range(8):
                    lt = b["xs"][:, par * 1024 + i * 128:par * 1024 + (i + 1) * 128]
                    mm = te.matmul(ps[:, bq * 512:bq * 512 + 512], lt,
                                   b["wq"][:, i * 512:(i + 1) * 512],
                                   start=(i == 0), stop=(i == 7))
                s.inc(mm, "smm")
                s.wait(ve, "smm", s.val("smm"))
                V.op(ve.tensor_copy, b["pscr"][:], ps[:, bq * 512:bq * 512 + 512])
                V.op(ve.tensor_copy, b["ttmp"][:, 512:1024], ps[:, bq * 512:bq * 512 + 512])
                costb = b["costm"][:, tt * 32:(tt + 1) * 32].unsqueeze(1).to_broadcast([128, 8, 32])
                sintb = b["sintm"][:, tt * 32:(tt + 1) * 32].unsqueeze(1).to_broadcast([128, 8, 32])
                rot_i = _rot_ops(V, ve, b["pscr"][:], b["ttmp"][:, 0:512], costb, sintb, b["rscr"])
                s.wait(te, "sve", rot_i)
                for j in range(4):
                    te.transpose(psb[:, par * 1024 + j * 128:par * 1024 + (j + 1) * 128],
                                 b["ttmp"][:, j * 128:(j + 1) * 128], b["idb"][:])
                tmm = None
                for j in range(4):
                    tmm = te.transpose(psb[:, par * 1024 + 512 + j * 128:par * 1024 + 512 + (j + 1) * 128],
                                       b["ttmp"][:, 512 + j * 128:512 + (j + 1) * 128], b["idb"][:])
                s.inc(tmm, "smm")
                s.wait(ve, "smm", s.val("smm"))
                for j in range(4):
                    V.op(ve.tensor_copy, b["qrot"][:, j * 512 + sub * 128:j * 512 + (sub + 1) * 128],
                         psb[:, par * 1024 + j * 128:par * 1024 + (j + 1) * 128])
                for j in range(4):
                    V.op(ve.tensor_copy, b["qpl"][:, j * 512 + sub * 128:j * 512 + (sub + 1) * 128],
                         psb[:, par * 1024 + 512 + j * 128:par * 1024 + 512 + (j + 1) * 128])
            V.op(ve.memset, b["den"][:], 0.0)
            s.wait(te, "sve", s.val("sve"))
            s.wait(sc, "smm", s.val("smm"))
            nact = 0
            for h in range(8):
                r0 = (h % 2) * 64
                of = h // 2
                for qsub in range(4):
                    lt = b["qpl"][r0:r0 + 64, of * 512 + qsub * 128:of * 512 + (qsub + 1) * 128]
                    for pr in range(4):
                        bk0 = (pr % 2) * 2
                        if nact >= 2:
                            s.wait(te, "ssc", s.val("ssc") - 1)
                        mm = None
                        for j in range(2):
                            kt = pr * 2 + j
                            rh = b["kpl"][r0:r0 + 64, of * 4096 + kt * 512:of * 4096 + (kt + 1) * 512]
                            mm = te.matmul(ps[:, (bk0 + j) * 512:(bk0 + j) * 512 + 512],
                                           lt, rh, start=True, stop=True)
                        s.inc(mm, "smm")
                        s.wait(sc, "smm", s.val("smm"))
                        if nact > 0:
                            s.wait(sc, "sve", s.val("sve"))
                        aidx = S.op(sc.activation, b["e1"][:, (pr % 2) * 1024:(pr % 2) * 1024 + 1024],
                                    ps[:, bk0 * 512:bk0 * 512 + 1024], AF.Exp, scale=0.125,
                                    accum_out=b["recrep"][:, 0:1])
                        nact += 1
                        s.wait(ve, "ssc", aidx)
                        V.op(ve.tensor_tensor, b["den"][:, h * 4 + qsub:h * 4 + qsub + 1],
                             b["den"][:, h * 4 + qsub:h * 4 + qsub + 1],
                             b["recrep"][:, 0:1], ALU.add)
            s.wait(ve, "smm", s.val("smm"))
            rec_i = V.op(ve.reciprocal, b["rec"][:], b["den"][:])
            for h in range(8):
                r0 = (h % 2) * 64
                of = h // 2
                nsub = (qt + 1) * 4
                npr = nsub // 2
                av_pair = {}
                exp_pair = {}
                slot_mask = {0: 0, 1: 0}
                first_av = True
                for pr in range(npr):
                    bk0 = (pr % 2) * 2
                    if pr >= 2:
                        s.wait(te, "ssc", exp_pair[pr - 2])
                    else:
                        s.wait(te, "ssc", s.val("ssc"))
                    mm = None
                    for j in range(2):
                        sub = pr * 2 + j
                        lt = b["krot"][r0:r0 + 64, of * 4096 + sub * 128:of * 4096 + (sub + 1) * 128]
                        rh = b["qrot"][r0:r0 + 64, of * 512:(of + 1) * 512]
                        mm = te.matmul(ps[:, (bk0 + j) * 512:(bk0 + j) * 512 + 512],
                                       lt, rh, start=True, stop=True)
                    s.inc(mm, "smm")
                    s.wait(sc, "smm", s.val("smm"))
                    if pr >= 2:
                        s.wait(sc, "smm", av_pair[pr - 2])
                    if slot_mask[pr % 2]:
                        s.wait(sc, "sve", slot_mask[pr % 2])
                    eidx = S.op(sc.activation, b["e1"][:, (pr % 2) * 1024:(pr % 2) * 1024 + 1024],
                                ps[:, bk0 * 512:bk0 * 512 + 1024], AF.Exp, scale=0.125)
                    exp_pair[pr] = eidx
                    mi = eidx
                    for j in range(2):
                        sub = pr * 2 + j
                        if sub >= qt * 4:
                            mj = sub - qt * 4
                            s.wait(ve, "ssc", eidx)
                            mi = V.op(ve.tensor_tensor,
                                      b["e1"][:, (pr % 2) * 1024 + j * 512:(pr % 2) * 1024 + (j + 1) * 512],
                                      b["e1"][:, (pr % 2) * 1024 + j * 512:(pr % 2) * 1024 + (j + 1) * 512],
                                      b["masks"][:, mj * 512:(mj + 1) * 512], ALU.mult)
                            slot_mask[pr % 2] = mi
                    s.wait(te, "ssc", eidx)
                    if mi != eidx:
                        s.wait(te, "sve", mi)
                    av = None
                    for j in range(2):
                        sub = pr * 2 + j
                        av = te.matmul(ps[0:64, 4 * 512:4 * 512 + 512],
                                       b["vtm"][:, sub * 512 + h * 64:sub * 512 + (h + 1) * 64],
                                       b["e1"][:, (pr % 2) * 1024 + j * 512:(pr % 2) * 1024 + (j + 1) * 512],
                                       start=first_av, stop=(pr == npr - 1 and j == 1))
                        first_av = False
                    av_pair[pr] = s.inc(av, "smm")
                s.wait(te, "sve", rec_i)
                rb = None
                for qsub in range(4):
                    c = h * 4 + qsub
                    rb = te.matmul(ps[0:64, 5 * 512 + qsub * 128:5 * 512 + (qsub + 1) * 128],
                                   b["rec"][:, c:c + 1].to_broadcast([128, 64]), b["idf"][:],
                                   start=True, stop=True)
                s.inc(rb, "smm")
                s.wait(ve, "smm", s.val("smm"))
                V.op(ve.tensor_copy, b["recrep"][0:64, 0:512], ps[0:64, 5 * 512:5 * 512 + 512])
                yi = V.op(ve.tensor_tensor, b["yg"][r0:r0 + 64, of * 4096 + qt * 512:of * 4096 + (qt + 1) * 512],
                          ps[0:64, 4 * 512:4 * 512 + 512], b["recrep"][0:64, 0:512], ALU.mult)
                s.wait(te, "sve", yi)

        xsrc = x_in if g == 0 else x_out
        xprv = b["e1"][:].bitcast(F32)
        xov = b["qpl"][:].bitcast(F32)
        s.wait(sy, "sve", s.val("sve"))
        s.wait(sy, "sy", s.val("sy"))
        s.wait(sy, "soA", s.val("soA"))
        s.wait(sy, "soB", s.val("soB"))
        mm_d = []
        ve_d = []
        sy_d = []
        dcnt = 0
        for of in range(8):
            par = of % 2
            if of >= 2:
                s.wait(gp, "smm", mm_d[(of - 2) * 8 + 7])
            else:
                s.wait(gp, "smm", s.val("smm"))
            wsm = "sxA" if par == 0 else "sxB"
            s.inc(gp.dma_start(
                b["woc"][:, par * 512:(par + 1) * 512].rearrange("p (i c) -> p i c", i=4),
                wout[g * 512:(g + 1) * 512, of * 128:(of + 1) * 128]
                .rearrange("(i p) c -> p i c", p=128)), wsm, 16)
            s.wait(te, wsm, s.val(wsm))
            for qt in range(8):
                qpar = dcnt % 2
                if dcnt >= 2:
                    s.wait(sy, "sve", ve_d[dcnt - 2])
                xsem = "syA" if qpar == 0 else "syB"
                s.inc(sy.dma_start(xprv[:, qpar * 512:(qpar + 1) * 512],
                                   xsrc[of * 128:(of + 1) * 128, qt * 512:(qt + 1) * 512]),
                      xsem, 16)
                if dcnt >= 2:
                    s.wait(te, "sve", ve_d[dcnt - 2])
                bk = qpar
                mm = None
                for i in range(4):
                    mm = te.matmul(ps[:, bk * 512:bk * 512 + 512],
                                   b["woc"][:, par * 512 + i * 128:par * 512 + (i + 1) * 128],
                                   b["yg"][:, i * 4096 + qt * 512:(i) * 4096 + (qt + 1) * 512],
                                   start=(i == 0), stop=(i == 3))
                mm_d.append(s.inc(mm, "smm"))
                s.wait(ve, "smm", mm_d[-1])
                s.wait(ve, "syA" if qpar == 0 else "syB", s.val("syA" if qpar == 0 else "syB"))
                if dcnt >= 2:
                    s.wait(ve, "soA" if qpar == 0 else "soB",
                           sy_d[dcnt - 2])
                if g == 0:
                    vv = V.op(ve.scalar_tensor_tensor, xov[:, qpar * 512:(qpar + 1) * 512],
                              ps[:, bk * 512:bk * 512 + 512], b["boutc"][:, of:of + 1],
                              xprv[:, qpar * 512:(qpar + 1) * 512], ALU.add, ALU.add)
                else:
                    vv = V.op(ve.tensor_tensor, xov[:, qpar * 512:(qpar + 1) * 512],
                              ps[:, bk * 512:bk * 512 + 512],
                              xprv[:, qpar * 512:(qpar + 1) * 512], ALU.add)
                ve_d.append(vv)
                s.wait(sy, "sve", vv)
                sy_d.append(s.inc(sy.dma_start(
                    x_out[of * 128:(of + 1) * 128, qt * 512:(qt + 1) * 512],
                    xov[:, qpar * 512:(qpar + 1) * 512]), "soA" if qpar == 0 else "soB", 16))
                dcnt += 1
        s.wait(sc, "soA", s.val("soA"))
        s.wait(sc, "soB", s.val("soB"))
        s.wait(ve, "soA", s.val("soA"))
        s.wait(ve, "soB", s.val("soB"))


def build_l0attn_launch():
    nc = bass.Bass("TRN2", target_bir_lowering=False)
    x_in = nc.dram_tensor("x", [D, T], F32, kind="ExternalInput").ap()
    wqkv = nc.dram_tensor("wqkv", [D, 3 * D], F32, kind="ExternalInput").ap()
    wout = nc.dram_tensor("wout", [D, D], F32, kind="ExternalInput").ap()
    boutc_in = nc.dram_tensor("boutc_in", [128, 8], F32, kind="ExternalInput").ap()
    costm_in = nc.dram_tensor("costm_in", [128, 1024], F32, kind="ExternalInput").ap()
    sintm_in = nc.dram_tensor("sintm_in", [128, 1024], F32, kind="ExternalInput").ap()
    idb_in = nc.dram_tensor("idb_in", [128, 128], BF16, kind="ExternalInput").ap()
    idf_in = nc.dram_tensor("idf_in", [128, 128], F32, kind="ExternalInput").ap()
    masks_in = nc.dram_tensor("masks_in", [128, 4 * 512], BF16, kind="ExternalInput").ap()
    ones128_in = nc.dram_tensor("ones128_in", [128, 1], BF16, kind="ExternalInput").ap()
    ones1_in = nc.dram_tensor("ones1_in", [1, 128], F32, kind="ExternalInput").ap()
    cst_in = nc.dram_tensor("cst_in", [128, 8], F32, kind="ExternalInput").ap()
    x_out = nc.dram_tensor("xo", [D, T], F32, kind="ExternalOutput").ap()
    xn_d = nc.dram_tensor("xn_d", [D, T], BF16).ap()

    ctx = []
    try:
        bufs = alloc_l0attn_bufs(nc, ctx)
        s = Sems(nc, ["sx", "sxA", "sxB", "sy", "syA", "syB", "soA", "soB", "smm", "ssc", "sve", "spre"])
        for name, src in (("boutc", boutc_in), ("costm", costm_in), ("sintm", sintm_in),
                          ("idb", idb_in), ("idf", idf_in), ("masks", masks_in),
                          ("ones128", ones128_in), ("ones1", ones1_in), ("cst", cst_in)):
            s.inc(nc.gpsimd.dma_start(bufs[name][:], src), "spre", 16)
        for eng in (nc.sync, nc.tensor, nc.scalar, nc.vector, nc.gpsimd):
            s.wait(eng, "spre")
        emit_l0attn(nc, s, bufs, x_in, x_out, xn_d, wqkv, wout)
        s.wait(nc.sync, "syA")
        s.wait(nc.sync, "syB")
        s.wait(nc.gpsimd, "sx")
        s.wait(nc.gpsimd, "sxA")
        s.wait(nc.gpsimd, "sxB")
        s.wait(nc.sync, "sy")
        s.wait(nc.sync, "soA")
        s.wait(nc.sync, "soB")
    finally:
        for cm in reversed(ctx):
            cm.__exit__(None, None, None)
    return nc


def _l0attn_host_inputs(p):
    import ml_dtypes
    dh = 64
    i = np.arange(dh // 2, dtype=np.float32)
    theta = np.power(np.float32(10000.0), -2.0 * (i - 1.0) / dh).astype(np.float32)
    ang = np.arange(T, dtype=np.float32)[:, None] * theta[None, :]
    cos = np.cos(ang).astype(np.float32)
    sin = np.sin(ang).astype(np.float32)
    costm = cos.reshape(32, 128, 32).transpose(1, 0, 2).reshape(128, 1024)
    sintm = sin.reshape(32, 128, 32).transpose(1, 0, 2).reshape(128, 1024)
    masks = np.zeros((128, 4, 512), dtype=ml_dtypes.bfloat16)
    for j in range(4):
        for pp in range(128):
            masks[pp, j, j * 128 + pp:] = 1.0
    out = {
        "wqkv": np.ascontiguousarray(p["L0_attn_Wqkv"]),
        "wout": np.ascontiguousarray(p["L0_attn_Wout"]),
        "boutc_in": np.ascontiguousarray(p["L0_attn_bout"].reshape(8, 128).T),
        "costm_in": np.ascontiguousarray(costm),
        "sintm_in": np.ascontiguousarray(sintm),
        "idb_in": np.eye(128, dtype=ml_dtypes.bfloat16),
        "idf_in": np.eye(128, dtype=np.float32),
        "masks_in": np.ascontiguousarray(masks.reshape(128, 2048)),
        "ones128_in": np.ones((128, 1), dtype=ml_dtypes.bfloat16),
        "ones1_in": np.ones((1, 128), dtype=np.float32),
    }
    cst = np.zeros((128, 8), dtype=np.float32)
    cst[:, 0] = 1e-5
    out["cst_in"] = cst
    return out


def run_l0attn(xT, p):
    nc = build_l0attn_launch()
    shared = _l0attn_host_inputs(p)
    in_maps = [dict(shared, x=np.ascontiguousarray(xT[bb])) for bb in range(8)]
    res = run_bass_kernel_spmd(nc, in_maps, core_ids=list(range(8)))
    return np.stack([r["xo"] for r in res.results], axis=0)




L1_NST = 256


def alloc_l1attn_bufs(nc, ctx_list):
    def t(name, shape, dt):
        cm = nc.sbuf_tensor(name, shape, dt)
        h = cm.__enter__()
        ctx_list.append(cm)
        return h

    bufs = dict(
        fmb=t("fmb", [128, 4104], BF16),
        cA=t("cA", [128, 4096], F32),
        cB=t("cB", [128, 4096], F32),
        tmcol=t("tmcol", [128, 32 * 128], BF16),
        tm2=t("tm2", [128, 32 * 128], BF16),
        sqs=t("sqs", [128, 4096], F32),
        nrm=t("nrm", [128, 64], F32),
        lrf=t("lrf", [32, 4096], F32),
        lrtm=t("lrtm", [128, 32 * 32], F32),
        lrio=t("lrio", [128, 4 * 256], F32),
        p1m=t("p1m", [32, 4 * 128], F32),
        e8m=t("e8m", [8, 128], F32),
        convc=t("convc", [128, 3 * 32], F32),
        w3p1=t("w3p1", [128, 3 * 8], F32),
        wcols=t("wcols", [128, 2 * 1024], BF16),
        xsf=t("xsf", [128, 2 * 4096], BF16),
        idb=t("idb", [128, 128], BF16),
        idf=t("idf", [128, 128], F32),
        qs=t("qs", [128, 3 * 512], BF16),
        bdc=t("bdc", [128, 128], BF16),
        bdo=t("bdo", [128, 128], BF16),
        win0=t("win0", [128, 2 * 256], F32),
        win1=t("win1", [128, 2 * 256], F32),
        wout0=t("wout0", [128, 2 * 256], F32),
        wout1=t("wout1", [128, 2 * 256], F32),
        winit=t("winit", [128, 4 * 256], F32),
        scr1=t("scr1", [128, 2 * 512], F32),
        scr2=t("scr2", [128, 2 * 512], F32),
        scrs=t("scrs", [128, 2 * 64], F32),
        rb=t("rb", [128, 2 * 520], BF16),
        oring=t("oring", [128, 2 * 2 * 8 * 64], F32),
        otile=t("otile", [128, 1024], F32),
        ogfm=t("ogfm", [128, 8 * 512], BF16),
        grep=t("grep", [128, 128], F32),
        sums=t("sums", [128, 64], F32),
        woc=t("woc", [128, 2 * 1024], BF16),
        xpr=t("xpr", [128, 2 * 512], F32),
        xov=t("xov", [128, 2 * 512], F32),
        lngb=t("lngb", [1, 128], F32),
        ones128=t("ones128", [128, 1], BF16),
        ones1=t("ones1", [1, 128], F32),
        cst=t("cst", [128, 8], F32),
    )
    cab = bufs["cA"][:].bitcast(BF16)
    bufs["xbf"] = cab[:, 0:4096]
    bufs["xnout"] = cab[:, 4096:8192]
    cbb = bufs["cB"][:].bitcast(BF16)
    bufs["x2f"] = cbb[:, 0:1024]
    bufs["rows"] = bufs["cB"][0:1, 1024:3584]
    bufs["pscr"] = bufs["cB"][:, 3584:4096]
    xsfv = bufs["xsf"][:].bitcast(F32)
    bufs["o2t"] = xsfv[:, 1024:2048]
    bufs["rgt"] = xsfv[:, 2048:3072]
    bufs["gt"] = bufs["xsf"][:, 6144:7168]
    bufs["ogtm"] = bufs["xsf"][:, 7168:8192]
    cm = nc.psum_tensor("ps", [128, 6 * 512], F32)
    bufs["ps"] = cm.__enter__()
    ctx_list.append(cm)
    cm = nc.psum_tensor("psb", [128, 2 * 1024], BF16)
    bufs["psb"] = cm.__enter__()
    ctx_list.append(cm)
    return bufs


def emit_ln2(nc, s, bufs, x_in, xn_d):
    gp, sy, te, sc, ve = nc.gpsimd, nc.sync, nc.tensor, nc.scalar, nc.vector
    V = Chain(s, ve, "sve")
    S = Chain(s, sc, "ssc")
    xbf, x2f, xnout = bufs["xbf"], bufs["x2f"], bufs["xnout"]
    rows = bufs["rows"]
    ones128, ones1, ps = bufs["ones128"], bufs["ones1"], bufs["ps"]
    for tt in range(8):
        c0 = tt * 512
        if tt > 0:
            s.wait(gp, "sve", s.val("sve"))
        sx_here = s.inc(gp.dma_start(
            xbf[:].rearrange("p (i c) -> p i c", i=8),
            x_in[:, c0:c0 + 512].rearrange("(i p) c -> p i c", p=128)), "sx", 16)
        m_row = rows[0:1, 0:512]
        tmpa = rows[0:1, 512:1024]
        tmpb = rows[0:1, 1024:1536]
        s_row = rows[0:1, 1536:2048]
        b_row = rows[0:1, 2048:2560]
        s.wait(sc, "sx", sx_here)
        s.wait(te, "sx", sx_here)
        for ft in range(NFT):
            slot = ft % 2
            if ft >= 2:
                s.wait(sc, "smm", s.val("smm"))
            sqi = S.op(sc.activation, x2f[:, slot * 512:slot * 512 + 512],
                       xbf[:, ft * 512:(ft + 1) * 512], AF.Square)
            te.matmul(ps[0:1, 4 * 512:4 * 512 + 512], ones128[:],
                      xbf[:, ft * 512:(ft + 1) * 512],
                      start=(ft == 0), stop=(ft == NFT - 1))
            s.wait(te, "ssc", sqi)
            s.inc(te.matmul(ps[0:1, 5 * 512:5 * 512 + 512], ones128[:],
                            x2f[:, slot * 512:slot * 512 + 512],
                            start=(ft == 0), stop=(ft == NFT - 1)), "smm")
        s.wait(ve, "smm", s.val("smm"))
        V.op(ve.tensor_scalar, m_row, ps[0:1, 4 * 512:4 * 512 + 512], 1.0 / D, None, ALU.mult)
        V.op(ve.tensor_tensor, tmpa, m_row, m_row, ALU.mult)
        vi = V.op(ve.scalar_tensor_tensor, tmpb, ps[0:1, 5 * 512:5 * 512 + 512],
                  1.0 / D, tmpa, ALU.mult, ALU.subtract)
        s.wait(sc, "sve", vi)
        S.op(sc.activation, tmpa, tmpb, AF.Ln, bias=bufs["cst"][0:1, 0:1])
        si = S.op(sc.activation, s_row, tmpa, AF.Exp, scale=-0.5)
        s.wait(ve, "ssc", si)
        bi = V.op(ve.scalar_tensor_tensor, b_row, m_row, -1.0, s_row, ALU.mult, ALU.mult)
        s.wait(te, "sve", bi)
        te.matmul(ps[:, 2 * 512:2 * 512 + 512], ones1[:], s_row, start=True, stop=True)
        s.inc(te.matmul(ps[:, 3 * 512:3 * 512 + 512], ones1[:], b_row,
                        start=True, stop=True), "smm")
        s.wait(ve, "smm", s.val("smm"))
        if tt > 0:
            s.wait(ve, "sy", s.val("sy"))
        for ft in range(NFT):
            V.op(ve.tensor_tensor, bufs["pscr"][:, 0:512],
                 xbf[:, ft * 512:(ft + 1) * 512], ps[:, 2 * 512:2 * 512 + 512], ALU.mult)
            V.op(ve.tensor_tensor, xnout[:, ft * 512:(ft + 1) * 512],
                 bufs["pscr"][:, 0:512], ps[:, 3 * 512:3 * 512 + 512], ALU.add)
        s.wait(sy, "sve", s.val("sve"))
        s.inc(sy.dma_start(
            xn_d[:, c0:c0 + 512].rearrange("(i p) c -> p i c", i=8),
            xnout.rearrange("p (i c) -> p i c", i=8)), "sy", 16)


def emit_l1attn(nc, s, bufs, x_in, x_out, xn_d, W, dscr):
    gp, sy, te, sc, ve = nc.gpsimd, nc.sync, nc.tensor, nc.scalar, nc.vector
    V = Chain(s, ve, "sve")
    S = Chain(s, sc, "ssc")
    b = bufs
    ps, psb = b["ps"], b["psb"]
    psbv = psb[:].bitcast(F32)
    qkvl, gate_d, od = dscr["qkvl"], dscr["gate"], dscr["od"]

    emit_ln2(nc, s, bufs, x_in, xn_d)
    s.wait(gp, "sy", s.val("sy"))

    V.op(ve.tensor_scalar, b["w3p1"][:],
         bass.AP(b["convc"], 3, [[96, 128], [32, 3], [4, 8]]),
         1.0, None, ALU.add)
    te.matmul(ps[:, 0:64], b["ones1"][:], b["lngb"][0:1, 0:64], start=True, stop=True)
    s.inc(te.matmul(ps[:, 512:512 + 64], b["ones1"][:], b["lngb"][0:1, 64:128],
                    start=True, stop=True), "smm")
    s.wait(ve, "smm", s.val("smm"))
    V.op(ve.tensor_copy, b["grep"][:, 0:64], ps[:, 0:64])
    V.op(ve.tensor_copy, b["grep"][:, 64:128], ps[:, 512:512 + 64])

    s.inc(gp.dma_start(b["wcols"][:, 0:256].rearrange("p (i c) -> p i c", i=8),
                       W["wlrT"].rearrange("(i p) c -> p i c", p=128)), "sx", 16)
    sx_lr = s.val("sx")
    for tc in range(8):
        par = tc % 2
        xsem = "syA" if par == 0 else "syB"
        if tc >= 2:
            s.wait(sy, "smm", s.val("smm"))
        else:
            s.wait(sy, "sy", s.val("sy"))
        s.inc(sy.dma_start(
            b["xsf"][:, par * 4096:(par + 1) * 4096].rearrange("p (i c) -> p i c", i=8),
            xn_d[:, tc * 512:(tc + 1) * 512].rearrange("(i p) c -> p i c", p=128)),
            xsem, 16)
        s.wait(te, xsem, s.val(xsem))
        s.wait(te, "sx", sx_lr)
        s.wait(te, "sve", s.val("sve"))
        mm = None
        for i in range(8):
            mm = te.matmul(ps[0:32, par * 512:par * 512 + 512],
                           b["wcols"][:, i * 32:(i + 1) * 32],
                           b["xsf"][:, par * 4096 + i * 512:par * 4096 + (i + 1) * 512],
                           start=(i == 0), stop=(i == 7))
        s.inc(mm, "smm")
        s.wait(ve, "smm", s.val("smm"))
        V.op(ve.tensor_scalar, b["lrf"][0:32, tc * 512:(tc + 1) * 512],
             ps[0:32, par * 512:par * 512 + 512], 0.001, None, ALU.add)
    s.wait(sc, "sve", s.val("sve"))
    S.op(sc.activation, b["sqs"][0:32, 0:4096], b["lrf"][0:32, :], AF.Abs)
    S.op(sc.activation, b["sqs"][0:32, 0:4096], b["sqs"][0:32, 0:4096], AF.Exp, scale=-1.0)
    li = S.op(sc.activation, b["sqs"][0:32, 0:4096], b["sqs"][0:32, 0:4096],
              AF.Ln, bias=b["cst"][0:32, 1:2])
    s.wait(ve, "ssc", li)
    V.op(ve.tensor_scalar, b["cA"][0:32, 0:4096], b["lrf"][0:32, :], 0.0, None, ALU.max)
    lr_i = V.op(ve.tensor_tensor, b["lrf"][0:32, :], b["cA"][0:32, 0:4096],
                b["sqs"][0:32, 0:4096], ALU.add)
    s.wait(te, "sve", lr_i)
    for j in range(4):
        s.inc(te.matmul(ps[:, j * 512:j * 512 + 256],
                        b["p1m"][:, j * 128:(j + 1) * 128],
                        bass.AP(b["lrf"], 0, [[4096, 32], [16, 256]]),
                        start=True, stop=True), "smm")
    s.wait(ve, "smm", s.val("smm"))
    for j in range(4):
        V.op(ve.tensor_copy, b["lrio"][:, j * 256:(j + 1) * 256],
             ps[:, j * 512:j * 512 + 256])
    for blk in range(32):
        bnk = 4 + (blk % 2)
        if blk >= 2:
            s.wait(te, "sve", s.val("sve"))
        s.inc(te.transpose(ps[:, bnk * 512:bnk * 512 + 32],
                           b["lrf"][0:32, blk * 128:(blk + 1) * 128],
                           b["idf"][0:32, 0:32]), "smm")
        s.wait(ve, "smm", s.val("smm"))
        V.op(ve.tensor_copy, b["lrtm"][:, blk * 32:(blk + 1) * 32],
             ps[:, bnk * 512:bnk * 512 + 32])
    lrtm_i = s.val("sve")

    V.op(ve.memset, b["fmb"][:, 0:3], 0.0)
    WT = [W["wqT"], W["wkT"], W["wvT"], W["wgT"]]
    for ti in range(4):
        for ft in range(8):
            fpar = ft % 2
            s.wait(gp, "smm", s.val("smm"))
            s.inc(gp.dma_start(
                b["wcols"][:, fpar * 1024:fpar * 1024 + 1024].rearrange("p (i c) -> p i c", i=8),
                WT[ti][:, ft * 128:(ft + 1) * 128].rearrange("(i p) c -> p i c", p=128)),
                "sx", 16)
            sx_here = s.val("sx")
            for tc in range(8):
                par = tc % 2
                xsem = "syA" if par == 0 else "syB"
                s.wait(sy, "smm", s.val("smm"))
                s.inc(sy.dma_start(
                    b["xsf"][:, par * 4096:(par + 1) * 4096].rearrange("p (i c) -> p i c", i=8),
                    xn_d[:, tc * 512:(tc + 1) * 512].rearrange("(i p) c -> p i c", p=128)),
                    xsem, 16)
                s.wait(te, xsem, s.val(xsem))
                s.wait(te, "sx", sx_here)
                s.wait(te, "sve", s.val("sve"))
                mm = None
                for i in range(8):
                    mm = te.matmul(ps[:, par * 512:par * 512 + 512],
                                   b["wcols"][:, fpar * 1024 + i * 128:fpar * 1024 + (i + 1) * 128],
                                   b["xsf"][:, par * 4096 + i * 512:par * 4096 + (i + 1) * 512],
                                   start=(i == 0), stop=(i == 7))
                s.inc(mm, "smm")
                s.wait(ve, "smm", s.val("smm"))
                V.op(ve.tensor_copy, b["fmb"][:, 3 + tc * 512:3 + (tc + 1) * 512],
                     ps[:, par * 512:par * 512 + 512])
            if ti < 3:
                cc = lambda j: b["convc"][:, ti * 32 + ft * 4 + j:ti * 32 + ft * 4 + j + 1]
                V.op(ve.tensor_scalar, b["cA"][:], b["fmb"][:, 0:4096], cc(0), None, ALU.mult)
                V.op(ve.scalar_tensor_tensor, b["cB"][:], b["fmb"][:, 1:4097], cc(1),
                     b["cA"][:], ALU.mult, ALU.add)
                V.op(ve.scalar_tensor_tensor, b["cA"][:], b["fmb"][:, 2:4098], cc(2),
                     b["cB"][:], ALU.mult, ALU.add)
                ci = V.op(ve.scalar_tensor_tensor, b["cB"][:], b["fmb"][:, 3:4099],
                          b["w3p1"][:, ti * 8 + ft:ti * 8 + ft + 1], b["cA"][:],
                          ALU.mult, ALU.add)
                s.wait(sc, "sve", ci)
                si = S.op(sc.activation, b["fmb"][:, 3:4099], b["cB"][:], AF.Silu)
                s.wait(te, "ssc", si)
            else:
                s.wait(te, "sve", s.val("sve"))
            s.wait(ve, "sy", s.val("sy"))
            for tt in range(32):
                tpar = tt % 2
                if tt >= 2:
                    s.wait(te, "sve", s.val("sve"))
                s.inc(te.transpose(psb[:, tpar * 1024:tpar * 1024 + 128],
                                   b["fmb"][:, 3 + tt * 128:3 + (tt + 1) * 128],
                                   b["idb"][:]), "smm")
                s.wait(ve, "smm", s.val("smm"))
                V.op(ve.tensor_copy, b["tmcol"][:, tt * 128:(tt + 1) * 128],
                     psb[:, tpar * 1024:tpar * 1024 + 128])
            spill_src = b["tmcol"]
            if ti <= 1:
                s.wait(sc, "sve", s.val("sve"))
                si = S.op(sc.activation, b["sqs"][:], b["tmcol"][:], AF.Square)
                s.wait(ve, "ssc", si)
                V.op(ve.tensor_reduce, b["nrm"][:, 0:64],
                     b["sqs"][:].rearrange("p (a b) -> p a b", b=64), AX.X, ALU.add)
                sqi = V.op(ve.tensor_copy, b["sums"][:, 0:64], b["nrm"][:, 0:64])
                s.wait(sc, "sve", sqi)
                si2 = S.op(sc.activation, b["nrm"][:, 0:64], b["sums"][:, 0:64], AF.Sqrt)
                s.wait(ve, "ssc", si2)
                V.op(ve.reciprocal, b["sums"][:, 0:64], b["nrm"][:, 0:64])
                V.op(ve.tensor_tensor, b["tm2"][:], b["tmcol"][:],
                     bass.AP(b["sums"], 0, [[64, 128], [2, 32], [1, 2], [0, 64]]),
                     ALU.mult)
                spill_src = b["tm2"]
            sp_i = s.val("sve")
            s.wait(sy, "sve", sp_i)

            def spill(src_t, tau):
                g_ = ft // 4
                for h2 in range(2):
                    hp = 2 * (ft % 4) + h2
                    for j in range(8):
                        base = j * 65536 + hp * 16 * 512 + g_ * 256 + tau * 64
                        s.inc(sy.dma_start(
                            bass.AP(qkvl.tensor, base,
                                    [[512, 16], [524288, 32], [1, 64]]),
                            bass.AP(src_t, 16 * j * 4096 + h2 * 64,
                                    [[4096, 16], [128, 32], [1, 64]])), "sy", 16)

            if ti < 3:
                spill(spill_src, ti)
            if ti == 2:
                s.wait(ve, "sve", lrtm_i)
                vi = V.op(ve.tensor_tensor, b["tm2"][:], b["tmcol"][:],
                          bass.AP(b["lrtm"], 4 * ft + 1,
                                  [[1024, 128], [32, 32], [2, 2], [0, 64]]),
                          ALU.mult)
                s.wait(sy, "sve", vi)
                spill(b["tm2"], 3)
            if ti == 3:
                s.inc(sy.dma_start(
                    gate_d[:, ft * 128:(ft + 1) * 128].rearrange("(tt p) c -> p tt c", p=128),
                    b["tmcol"][:].rearrange("p (tt c) -> p tt c", c=128)), "sy", 16)

    s.wait(ve, "spre", s.val("spre"))
    V.op(ve.tensor_copy, b["win0"][:, 0:256], b["winit"][:, 0:256])
    V.op(ve.tensor_copy, b["wout0"][:, 0:256], b["winit"][:, 256:512])
    V.op(ve.tensor_copy, b["win0"][:, 256:512], b["winit"][:, 512:768])
    V.op(ve.tensor_copy, b["wout0"][:, 256:512], b["winit"][:, 768:1024])
    s.wait(sy, "sy", s.val("sy"))

    qs_reader = [0, 0, 0]
    oring_dma = [0, 0]
    win_par = [0, 0]
    wout_par = [0, 0]
    WIN = [b["win0"], b["win1"]]
    WOUT = [b["wout0"], b["wout1"]]
    QS = b["qs"]
    SC = b["scrs"]
    RB = b["rb"]

    def stream_step(c):
        sl = c % 3
        sem = ("sqA", "sqB", "sqC")[sl]
        if c >= 3:
            s.wait(sy, "sve", qs_reader[sl])
        dst = b["qs"][:, sl * 512:(sl + 1) * 512]
        src = bass.AP(qkvl.tensor, c * 65536, [[512, 128], [1, 512]])
        s.inc(sy.dma_start(dst, src), sem, 16)
        return s.val(sem)

    stream_sem = {0: stream_step(0), 1: stream_step(1), 2: stream_step(2)}

    for c in range(L1_NST):
        sl = c % 3
        sem = ("sqA", "sqB", "sqC")[sl]
        s.wait(ve, sem, stream_sem[c])
        orp = (c // 8) % 2
        if c % 8 == 0 and c >= 16:
            s.wait(ve, "soA" if orp == 0 else "soB", oring_dma[orp])
        for g in range(2):
            so = g * 64
            qoff = sl * 512 + g * 256
            win, wout = WIN[win_par[g]], WOUT[wout_par[g]]
            win_n, wout_n = WIN[1 - win_par[g]], WOUT[1 - wout_par[g]]
            bnk = g * 3
            pb = g * 512
            qk2 = bass.AP(QS, qoff, [[1536, 128], [64, 2], [0, 4], [1, 64]])
            winv = bass.AP(win, g * 256, [[512, 128], [0, 2], [1, 4], [4, 64]])
            V.op(ve.tensor_tensor,
                 b["scr1"][:, g * 512:g * 512 + 512].rearrange("p (t D d) -> p t D d", t=2, D=4),
                 qk2, winv, ALU.mult)
            V.op(ve.tensor_reduce, b["scrs"][:, so:so + 8],
                 b["scr1"][:, g * 512:g * 512 + 512].rearrange("p (t D d) -> p t D d", t=2, D=4),
                 AX.X, ALU.add)
            V.op(ve.tensor_reduce, b["scrs"][:, so + 8:so + 10],
                 b["scrs"][:, so:so + 8].rearrange("p (t D) -> p t D", t=2), AX.X, ALU.max)
            ei = V.op(ve.tensor_tensor,
                      b["scrs"][:, so + 16:so + 24].rearrange("p (t D) -> p t D", t=2),
                      b["scrs"][:, so:so + 8].rearrange("p (t D) -> p t D", t=2),
                      bass.AP(SC, so + 8, [[128, 128], [1, 2], [0, 4]]), ALU.subtract)
            s.wait(sc, "sve", ei)
            e1i = S.op(sc.activation, b["scrs"][:, so + 24:so + 32],
                       b["scrs"][:, so + 16:so + 24], AF.Exp)
            s.wait(ve, "ssc", e1i)
            V.op(ve.tensor_reduce, b["scrs"][:, so + 32:so + 34],
                 b["scrs"][:, so + 24:so + 32].rearrange("p (t D) -> p t D", t=2),
                 AX.X, ALU.add)
            V.op(ve.reciprocal, b["scrs"][:, so + 34:so + 36], b["scrs"][:, so + 32:so + 34])
            V.op(ve.tensor_scalar, b["scrs"][:, so + 36:so + 40],
                 b["scrs"][:, so + 24:so + 28], b["scrs"][:, so + 34:so + 35], None, ALU.mult)
            V.op(ve.tensor_scalar, b["scrs"][:, so + 40:so + 44],
                 b["scrs"][:, so + 28:so + 32], b["scrs"][:, so + 35:so + 36], None, ALU.mult)
            kvi = V.op(ve.tensor_tensor,
                       bass.AP(RB, g * 520, [[1040, 128], [1, 64], [0, 4]]).rearrange(
                           "p d D -> p d D") if False else
                       bass.AP(RB, g * 520, [[1040, 128], [4, 64], [1, 4]]),
                       bass.AP(SC, so + 40, [[128, 128], [0, 64], [1, 4]]),
                       bass.AP(QS, qoff + 192, [[1536, 128], [1, 64], [0, 4]]),
                       ALU.mult)
            s.wait(te, "sve", kvi)
            te.matmul(ps[:, bnk * 512:bnk * 512 + 256], b["bdc"][:],
                      b["rb"][:, g * 520:g * 520 + 256], start=True, stop=True)
            s.inc(te.matmul(ps[:, (bnk + 1) * 512:(bnk + 1) * 512 + 256], b["bdo"][:],
                            b["rb"][:, g * 520:g * 520 + 256], start=True, stop=True), "smm")
            s.wait(ve, "smm", s.val("smm"))
            V.op(ve.tensor_tensor, b["scr1"][:, g * 512:g * 512 + 256],
                 ps[:, bnk * 512:bnk * 512 + 256], win.tensor[:, 0:1] if False else
                 wout[:, g * 256:(g + 1) * 256], ALU.add)
            V.op(ve.tensor_tensor,
                 b["scr2"][:, g * 512:g * 512 + 256].rearrange("p (d D) -> p d D", d=64),
                 b["scr1"][:, g * 512:g * 512 + 256].rearrange("p (d D) -> p d D", d=64),
                 bass.AP(SC, so + 36, [[128, 128], [0, 64], [1, 4]]), ALU.mult)
            orw = g * 1024 + orp * 512 + (c % 8) * 64
            V.op(ve.tensor_reduce, b["oring"][:, orw:orw + 64],
                 b["scr2"][:, g * 512:g * 512 + 256].rearrange("p (d D) -> p d D", d=64),
                 AX.X, ALU.add)
            V.op(ve.tensor_tensor, wout_n[:, g * 256:(g + 1) * 256],
                 ps[:, (bnk + 1) * 512:(bnk + 1) * 512 + 256],
                 wout[:, g * 256:(g + 1) * 256], ALU.add)
            wout_par[g] = 1 - wout_par[g]
            wout = WOUT[wout_par[g]]
            wout_n = WOUT[1 - wout_par[g]]
            for it in range(2):
                winv2 = bass.AP(win, g * 256, [[512, 128], [1, 4], [4, 64]])
                woutv2 = bass.AP(wout, g * 256, [[512, 128], [1, 4], [4, 64]])
                V.op(ve.tensor_tensor,
                     b["scr1"][:, g * 512:g * 512 + 256].rearrange("p (D d) -> p D d", D=4),
                     bass.AP(QS, qoff + 64, [[1536, 128], [0, 4], [1, 64]]),
                     winv2, ALU.mult)
                V.op(ve.tensor_tensor,
                     b["scr1"][:, g * 512 + 256:g * 512 + 512].rearrange("p (D d) -> p D d", D=4),
                     bass.AP(QS, qoff + 128, [[1536, 128], [0, 4], [1, 64]]),
                     woutv2, ALU.mult)
                sv = V.op(ve.tensor_reduce, b["scrs"][:, so + 44:so + 52],
                          b["scr1"][:, g * 512:g * 512 + 512].rearrange(
                              "p (t D d) -> p t D d", t=2, D=4), AX.X, ALU.add)
                s.wait(te, "sve", sv)
                s.inc(te.transpose(psbv[:, pb:pb + 8][0:8, :] if False else
                                   bass.AP(psb.tensor, 2 * pb, [[2048, 8], [2, 128]]).bitcast(BF16)
                                   if False else psbv[0:8, pb:pb + 128],
                                   b["scrs"][:, so + 44:so + 52], b["idf"][:]), "smm")
                s.wait(ve, "smm", s.val("smm"))
                nmv = V.op(ve.tensor_reduce, b["scrs"][0:8, so + 56:so + 64],
                           bass.AP(psb.tensor, pb * 2, [[2048, 8], [1, 8], [8, 16]]).bitcast(F32)
                           if False else
                           psbv[0:8, pb:pb + 128].rearrange("p (h l) -> p h l", l=16),
                           AX.X, ALU.max)
                s.wait(te, "sve", nmv)
                s.inc(te.transpose(psbv[0:8, pb + 128:pb + 136],
                                   b["scrs"][0:8, so + 56:so + 64],
                                   b["idf"][0:8, 0:8]), "smm")
                s.wait(ve, "smm", s.val("smm"))
                nt = V.op(ve.tensor_copy, b["nrm"][0:8, g * 8:g * 8 + 8],
                          psbv[0:8, pb + 128:pb + 136])
                s.wait(te, "sve", nt)
                s.inc(te.matmul(psbv[:, pb + 136:pb + 144], b["e8m"][:],
                                b["nrm"][0:8, g * 8:g * 8 + 8], start=True, stop=True), "smm")
                s.wait(ve, "smm", s.val("smm"))
                svi = V.op(ve.tensor_tensor, b["scrs"][:, so + 16:so + 24],
                           b["scrs"][:, so + 44:so + 52],
                           psbv[:, pb + 136:pb + 144], ALU.subtract)
                s.wait(sc, "sve", svi)
                e2i = S.op(sc.activation, b["rb"][:, g * 520 + 512:g * 520 + 520],
                           b["scrs"][:, so + 16:so + 24], AF.Exp, scale=0.125)
                s.wait(ve, "ssc", e2i)
                V.op(ve.tensor_tensor,
                     bass.AP(RB, g * 520, [[1040, 128], [4, 64], [1, 4]]),
                     bass.AP(RB, g * 520 + 512, [[1040, 128], [0, 64], [1, 4]]),
                     bass.AP(QS, qoff + 128, [[1536, 128], [1, 64], [0, 4]]),
                     ALU.mult)
                t2i = V.op(ve.tensor_tensor,
                           bass.AP(RB, g * 520 + 256, [[1040, 128], [4, 64], [1, 4]]),
                           bass.AP(RB, g * 520 + 516, [[1040, 128], [0, 64], [1, 4]]),
                           bass.AP(QS, qoff + 64, [[1536, 128], [1, 64], [0, 4]]),
                           ALU.mult)
                s.wait(te, "sve", t2i)
                te.matmul(ps[:, (bnk + 2) * 512:(bnk + 2) * 512 + 512], b["bdo"][:],
                          b["rb"][:, g * 520:g * 520 + 512], start=True, stop=True)
                s.inc(te.matmul(ps[:, (bnk + 1) * 512:(bnk + 1) * 512 + 8], b["bdo"][:],
                                b["rb"][:, g * 520 + 512:g * 520 + 520],
                                start=True, stop=True), "smm")
                s.wait(ve, "smm", s.val("smm"))
                V.op(ve.reciprocal, b["scrs"][:, so + 52:so + 60],
                     ps[:, (bnk + 1) * 512:(bnk + 1) * 512 + 8])
                V.op(ve.tensor_tensor,
                     b["scr2"][:, g * 512:g * 512 + 256].rearrange("p (d D) -> p d D", d=64),
                     ps[:, (bnk + 2) * 512:(bnk + 2) * 512 + 256].rearrange("p (d D) -> p d D", d=64),
                     bass.AP(SC, so + 52, [[128, 128], [0, 64], [1, 4]]), ALU.mult)
                V.op(ve.tensor_tensor,
                     b["scr2"][:, g * 512 + 256:g * 512 + 512].rearrange("p (d D) -> p d D", d=64),
                     ps[:, (bnk + 2) * 512 + 256:(bnk + 2) * 512 + 512].rearrange("p (d D) -> p d D", d=64),
                     bass.AP(SC, so + 56, [[128, 128], [0, 64], [1, 4]]), ALU.mult)
                V.op(ve.scalar_tensor_tensor, wout_n[:, g * 256:(g + 1) * 256],
                     b["scr2"][:, g * 512:g * 512 + 256],
                     b["lrio"][:, (g * 2 + 1) * 256 + c:(g * 2 + 1) * 256 + c + 1],
                     wout[:, g * 256:(g + 1) * 256], ALU.mult, ALU.add)
                V.op(ve.scalar_tensor_tensor, win_n[:, g * 256:(g + 1) * 256],
                     b["scr2"][:, g * 512 + 256:g * 512 + 512],
                     b["lrio"][:, (g * 2 + 0) * 256 + c:(g * 2 + 0) * 256 + c + 1],
                     win[:, g * 256:(g + 1) * 256], ALU.mult, ALU.add)
                win_par[g] = 1 - win_par[g]
                wout_par[g] = 1 - wout_par[g]
                win, wout = WIN[win_par[g]], WOUT[wout_par[g]]
                win_n, wout_n = WIN[1 - win_par[g]], WOUT[1 - wout_par[g]]
        qs_reader[sl] = s.val("sve")
        if c + 3 < L1_NST:
            stream_sem[c + 3] = stream_step(c + 3)
        if c % 8 == 7:
            orp2 = (c // 8) % 2
            s.wait(sy, "sve", s.val("sve"))
            for g in range(2):
                dst = bass.AP(od.tensor, (g * 256 + (c - 7)) * 128 * 64,
                              [[64, 128], [8192, 8], [1, 64]])
                src = bass.AP(b["oring"], g * 1024 + orp2 * 512,
                              [[2048, 128], [64, 8], [1, 64]])
                oring_dma[orp2] = s.inc(sy.dma_start(dst, src),
                                        "soA" if orp2 == 0 else "soB", 16)


    s.wait(sy, "soA", s.val("soA"))
    s.wait(sy, "soB", s.val("soB"))
    woT = W["woT"]
    dcnt = 0
    ve_d = []
    sy_d = []
    for qt in range(8):
        for sub in range(4):
            tt = qt * 4 + sub
            par = sub % 2
            xsem = "syA" if par == 0 else "syB"
            s.wait(sy, "sve", s.val("sve"))
            for ct in range(8):
                for gg in range(2):
                    src = bass.AP(od.tensor, (gg * 256 * 128 + (8 * tt + ct) * 128) * 64,
                                  [[64, 16], [1024, 8], [1, 64]])
                    dst = bass.AP(b["otile"], 16 * ct * 1024 + gg * 512,
                                  [[1024, 16], [64, 8], [1, 64]])
                    s.inc(sy.dma_start(dst, src), xsem, 16)
            s.inc(sy.dma_start(b["gt"][:], gate_d[tt * 128:(tt + 1) * 128, :]), xsem, 16)
            s.wait(sc, xsem, s.val(xsem))
            s.wait(ve, xsem, s.val(xsem))
            si = S.op(sc.activation, b["o2t"][:], b["otile"][:], AF.Square)
            V.op(ve.tensor_reduce, b["sums"][:, 0:16],
                 b["otile"][:].rearrange("p (h d) -> p h d", d=64), AX.X, ALU.add)
            s.wait(ve, "ssc", si)
            V.op(ve.tensor_reduce, b["sums"][:, 16:32],
                 b["o2t"][:].rearrange("p (h d) -> p h d", d=64), AX.X, ALU.add)
            V.op(ve.tensor_scalar, b["sums"][:, 0:16], b["sums"][:, 0:16],
                 1.0 / 64, None, ALU.mult)
            V.op(ve.tensor_tensor, b["sums"][:, 32:48], b["sums"][:, 0:16],
                 b["sums"][:, 0:16], ALU.mult)
            vi = V.op(ve.scalar_tensor_tensor, b["sums"][:, 48:64], b["sums"][:, 16:32],
                      1.0 / 64, b["sums"][:, 32:48], ALU.mult, ALU.subtract)
            s.wait(sc, "sve", vi)
            si2 = S.op(sc.activation, b["sums"][:, 32:48], b["sums"][:, 48:64],
                       AF.Sqrt, bias=b["cst"][:, 0:1])
            s.wait(ve, "ssc", si2)
            V.op(ve.reciprocal, b["sums"][:, 48:64], b["sums"][:, 32:48])
            V.op(ve.tensor_tensor, b["rgt"][:].rearrange("p (h d) -> p h d", d=64),
                 bass.AP(b["sums"], 48, [[64, 128], [1, 16], [0, 64]]),
                 bass.AP(b["grep"], 0, [[128, 128], [0, 16], [1, 64]]),
                 ALU.mult)
            V.op(ve.tensor_tensor, b["o2t"][:].rearrange("p (h d) -> p h d", d=64),
                 b["otile"][:].rearrange("p (h d) -> p h d", d=64),
                 bass.AP(b["sums"], 0, [[64, 128], [1, 16], [0, 64]]),
                 ALU.subtract)
            V.op(ve.tensor_tensor, b["rgt"][:], b["o2t"][:], b["rgt"][:], ALU.mult)
            V.op(ve.tensor_tensor, b["rgt"][:].rearrange("p (h d) -> p h d", d=64),
                 b["rgt"][:].rearrange("p (h d) -> p h d", d=64),
                 bass.AP(b["grep"], 64, [[128, 128], [0, 16], [1, 64]]),
                 ALU.add)
            ogi = V.op(ve.tensor_tensor, b["ogtm"][:], b["rgt"][:], b["gt"][:], ALU.mult)
            s.wait(te, "sve", ogi)
            for j in range(8):
                tpar = j % 2
                if j >= 2:
                    s.wait(te, "sve", s.val("sve"))
                s.inc(te.transpose(psb[:, tpar * 1024:tpar * 1024 + 128],
                                   b["ogtm"][:, j * 128:(j + 1) * 128], b["idb"][:]), "smm")
                s.wait(ve, "smm", s.val("smm"))
                V.op(ve.tensor_copy, b["ogfm"][:, j * 512 + sub * 128:j * 512 + (sub + 1) * 128],
                     psb[:, tpar * 1024:tpar * 1024 + 128])
        ogfm_i = s.val("sve")
        for of in range(8):
            par = of % 2
            s.wait(gp, "smm", s.val("smm"))
            wsm = "sxA" if par == 0 else "sxB"
            s.inc(gp.dma_start(
                b["woc"][:, par * 1024:par * 1024 + 1024].rearrange("p (i c) -> p i c", i=8),
                woT[:, of * 128:(of + 1) * 128].rearrange("(i p) c -> p i c", p=128)),
                wsm, 16)
            qpar = dcnt % 2
            xsem2 = "sqA" if qpar == 0 else "sqB"
            if dcnt >= 2:
                s.wait(sy, "sve", ve_d[dcnt - 2])
            s.inc(sy.dma_start(b["xpr"][:, qpar * 512:(qpar + 1) * 512],
                               x_in[of * 128:(of + 1) * 128, qt * 512:(qt + 1) * 512]),
                  xsem2, 16)
            s.wait(te, wsm, s.val(wsm))
            s.wait(te, "sve", ogfm_i)
            if dcnt >= 2:
                s.wait(te, "sve", ve_d[dcnt - 2])
            bk = 4 + qpar
            mm = None
            for i in range(8):
                mm = te.matmul(ps[:, bk * 512:bk * 512 + 512],
                               b["woc"][:, par * 1024 + i * 128:par * 1024 + (i + 1) * 128],
                               b["ogfm"][:, i * 512:(i + 1) * 512],
                               start=(i == 0), stop=(i == 7))
            mmi = s.inc(mm, "smm")
            s.wait(ve, "smm", mmi)
            s.wait(ve, xsem2, s.val(xsem2))
            if dcnt >= 2:
                s.wait(ve, "soA" if qpar == 0 else "soB", sy_d[dcnt - 2])
            vv = V.op(ve.tensor_tensor, b["xov"][:, qpar * 512:(qpar + 1) * 512],
                      ps[:, bk * 512:bk * 512 + 512],
                      b["xpr"][:, qpar * 512:(qpar + 1) * 512], ALU.add)
            ve_d.append(vv)
            s.wait(sy, "sve", vv)
            sy_d.append(s.inc(sy.dma_start(
                x_out[of * 128:(of + 1) * 128, qt * 512:(qt + 1) * 512],
                b["xov"][:, qpar * 512:(qpar + 1) * 512]),
                "soA" if qpar == 0 else "soB", 16))
            dcnt += 1


def build_l1attn_launch():
    nc = bass.Bass("TRN2", target_bir_lowering=False)
    x_in = nc.dram_tensor("x", [D, T], F32, kind="ExternalInput").ap()
    W = {}
    for k in ("wqT", "wkT", "wvT", "wgT", "woT"):
        W[k] = nc.dram_tensor(k, [D, D], F32, kind="ExternalInput").ap()
    W["wlrT"] = nc.dram_tensor("wlrT", [D, 32], F32, kind="ExternalInput").ap()
    ins = {}
    for name, shape, dt in (("convc_in", [128, 96], F32), ("p1m_in", [32, 512], F32),
                            ("winit_in", [128, 1024], F32), ("lngb_in", [1, 128], F32),
                            ("bdc_in", [128, 128], BF16), ("bdo_in", [128, 128], BF16),
                            ("idb_in", [128, 128], BF16), ("idf_in", [128, 128], F32),
                            ("e8m_in", [8, 128], F32),
                            ("ones128_in", [128, 1], BF16), ("ones1_in", [1, 128], F32),
                            ("cst_in", [128, 8], F32)):
        ins[name] = nc.dram_tensor(name, shape, dt, kind="ExternalInput").ap()
    x_out = nc.dram_tensor("xo", [D, T], F32, kind="ExternalOutput").ap()
    xn_d = nc.dram_tensor("xn_d", [D, T], BF16).ap()
    qkvl = nc.dram_tensor("qkvl", [4, T, D], BF16).ap()
    gate_d = nc.dram_tensor("gate_d", [T, D], BF16).ap()
    od = nc.dram_tensor("od", [2, 256, 128, 64], F32).ap()

    ctx = []
    try:
        bufs = alloc_l1attn_bufs(nc, ctx)
        s = Sems(nc, ["sx", "sxA", "sxB", "sy", "syA", "syB", "soA", "soB",
                      "sqA", "sqB", "sqC", "smm", "ssc", "sve", "spre"])
        for name, key in (("convc_in", "convc"), ("p1m_in", "p1m"), ("winit_in", "winit"),
                          ("lngb_in", "lngb"), ("bdc_in", "bdc"), ("bdo_in", "bdo"),
                          ("idb_in", "idb"), ("idf_in", "idf"), ("e8m_in", "e8m"),
                          ("ones128_in", "ones128"), ("ones1_in", "ones1"),
                          ("cst_in", "cst")):
            s.inc(nc.gpsimd.dma_start(bufs[key][:], ins[name]), "spre", 16)
        for eng in (nc.sync, nc.tensor, nc.scalar, nc.vector, nc.gpsimd):
            s.wait(eng, "spre")
        emit_l1attn(nc, s, bufs, x_in, x_out, xn_d, W,
                    dict(qkvl=qkvl, gate=gate_d, od=od))
        for sem in ("syA", "syB", "sqA", "sqB", "sqC", "sy", "soA", "soB"):
            s.wait(nc.sync, sem)
        for sem in ("sx", "sxA", "sxB"):
            s.wait(nc.gpsimd, sem)
    finally:
        for cm in reversed(ctx):
            cm.__exit__(None, None, None)
    return nc


def _l1attn_host_inputs(p):
    import ml_dtypes
    g = lambda n: np.asarray(p["L1_attn_" + n])
    out = {
        "wqT": np.ascontiguousarray(g("Wq").T),
        "wkT": np.ascontiguousarray(g("Wk").T),
        "wvT": np.ascontiguousarray(g("Wv").T),
        "wgT": np.ascontiguousarray(g("Wg").T),
        "woT": np.ascontiguousarray(g("Wo").T),
        "wlrT": np.ascontiguousarray(g("Wlr").T),
    }
    convc = np.zeros((128, 96), dtype=np.float32)
    for ti, nm in enumerate(("cq", "ck", "cv")):
        cw = g(nm)
        convc[:, ti * 32:(ti + 1) * 32] = cw.reshape(8, 128, 4).transpose(1, 0, 2).reshape(128, 32)
    out["convc_in"] = convc
    p1m = np.zeros((32, 4, 128), dtype=np.float32)
    for gg in range(2):
        for which in range(2):
            for pp in range(128):
                hprime = pp // 16
                p1m[2 * (8 * gg + hprime) + which, gg * 2 + which, pp] = 1.0
    out["p1m_in"] = np.ascontiguousarray(p1m.transpose(0, 1, 2).reshape(32, 512))
    wi = g("W_in_init")[0]
    wo = g("W_out_init")[0]
    winit = np.zeros((128, 4, 256), dtype=np.float32)
    for gg in range(2):
        for hprime in range(8):
            h = 8 * gg + hprime
            for l in range(16):
                pp = 16 * hprime + l
                winit[pp, gg * 2 + 0] = wi[:, h, :].T.reshape(256)
                winit[pp, gg * 2 + 1] = wo[:, h, :].T.reshape(256)
    out["winit_in"] = winit.reshape(128, 1024)
    lngb = np.zeros((1, 128), dtype=np.float32)
    lngb[0, 0:64] = g("ln_g")
    lngb[0, 64:128] = g("ln_b")
    out["lngb_in"] = lngb
    bdc = np.zeros((128, 128), dtype=ml_dtypes.bfloat16)
    bdo = np.zeros((128, 128), dtype=ml_dtypes.bfloat16)
    for l in range(16):
        for hp in range(8):
            for l2 in range(16):
                bdo[16 * hp + l, 16 * hp + l2] = 1.0
                if l <= l2:
                    bdc[16 * hp + l, 16 * hp + l2] = 1.0
    out["bdc_in"] = bdc
    out["bdo_in"] = bdo
    e8 = np.zeros((8, 128), dtype=np.float32)
    for pp in range(128):
        e8[pp // 16, pp] = 1.0
    out["e8m_in"] = e8
    out["idb_in"] = np.eye(128, dtype=ml_dtypes.bfloat16)
    out["idf_in"] = np.eye(128, dtype=np.float32)
    out["ones128_in"] = np.ones((128, 1), dtype=ml_dtypes.bfloat16)
    out["ones1_in"] = np.ones((1, 128), dtype=np.float32)
    cst = np.zeros((128, 8), dtype=np.float32)
    cst[:, 0] = 1e-5
    cst[:, 1] = 1.0
    out["cst_in"] = cst
    return out


def run_l1attn(xT, p):
    nc = build_l1attn_launch()
    shared = _l1attn_host_inputs(p)
    in_maps = [dict(shared, x=np.ascontiguousarray(xT[bb])) for bb in range(8)]
    res = run_bass_kernel_spmd(nc, in_maps, core_ids=list(range(8)))
    return np.stack([r["xo"] for r in res.results], axis=0)


FFN_SCL_COLS = 136


def alloc_ffn_bufs(nc, ctx_list):
    def t(name, shape, dt):
        cm = nc.sbuf_tensor(name, shape, dt)
        h = cm.__enter__()
        ctx_list.append(cm)
        return h

    bufs = dict(
        xbf=t("xbf", [128, 8 * 512], BF16),
        xpr=t("xpr", [128, 2 * QT], F32),
        xn=t("xn", [128, 8 * QT], BF16),
        hb=t("hb", [128, 32 * QT], BF16),
        gA=t("gA", [128, 16 * QT], BF16),
        gB=t("gB", [128, 16 * QT], BF16),
        wa=t("wa", [128, 2 * 8 * 128], BF16),
        wb2=t("wb2", [128, 2 * 8 * 128], BF16),
        wg=t("wg", [128, 2 * 32 * 128], BF16),
        x2f=t("x2f", [128, 2 * 512], BF16),
        sil=t("sil", [128, 2 * 512], F32),
        rows=t("rows", [1, 5 * 512], F32),
        scl=t("scl", [128, 2 * FFN_SCL_COLS], F32),
        ones128=t("ones128", [128, 1], BF16),
        cst=t("cst", [128, 8], F32),
        ones1=t("ones1", [1, 128], F32),
    )
    cm = nc.psum_tensor("ps", [128, 4096], F32)
    bufs["ps"] = cm.__enter__()
    ctx_list.append(cm)
    return bufs


def build_ffn_launch():
    nc = bass.Bass("TRN2", target_bir_lowering=False)
    x_in = nc.dram_tensor("x", [D, T], F32, kind="ExternalInput").ap()
    W = {k: nc.dram_tensor(k, shp, F32, kind="ExternalInput").ap()
         for k, shp in [("w1T", [D, FH]), ("w2T", [D, FH]),
                        ("h1T", [FH, FH]), ("h2T", [FH, FH]), ("w3T", [FH, D])]}
    sclin = nc.dram_tensor("scl_in", [128, FFN_SCL_COLS], F32, kind="ExternalInput").ap()
    ones128_in = nc.dram_tensor("ones128_in", [128, 1], BF16, kind="ExternalInput").ap()
    ones1_in = nc.dram_tensor("ones1_in", [1, 128], F32, kind="ExternalInput").ap()
    cst_in = nc.dram_tensor("cst_in", [128, 8], F32, kind="ExternalInput").ap()
    x_out = nc.dram_tensor("xo", [D, T], F32, kind="ExternalOutput").ap()

    ctx = []
    try:
        bufs = alloc_ffn_bufs(nc, ctx)
        s = Sems(nc, ["sx", "sxr", "sxo", "swA", "swB", "smm", "ssc", "sve", "spre"])
        s.inc(nc.gpsimd.dma_start(bufs["scl"][:, 0:FFN_SCL_COLS], sclin), "spre", 16)
        s.inc(nc.gpsimd.dma_start(bufs["ones128"][:], ones128_in), "spre", 16)
        s.inc(nc.gpsimd.dma_start(bufs["ones1"][:], ones1_in), "spre", 16)
        s.inc(nc.gpsimd.dma_start(bufs["cst"][:], cst_in), "spre", 16)
        for eng in (nc.sync, nc.tensor, nc.scalar, nc.vector, nc.gpsimd):
            s.wait(eng, "spre")
        emit_ffn(nc, s, bufs, x_in, x_out, W, 0)
        s.wait(nc.sync, "sxo")
    finally:
        for cm in reversed(ctx):
            cm.__exit__(None, None, None)
    return nc


def _ffn_host_inputs(p, layer):
    L = f"L{layer}_ffn_"
    out = {
        "w1T": np.ascontiguousarray(p[L + "w1_w"].T),
        "w2T": np.ascontiguousarray(p[L + "w2_w"].T),
        "h1T": np.ascontiguousarray(p[L + "h1_w"].T),
        "h2T": np.ascontiguousarray(p[L + "h2_w"].T),
        "w3T": np.ascontiguousarray(p[L + "w3_w"].T),
    }
    out["scl_in"] = _ffn_scl(p, layer)
    import ml_dtypes
    out["ones128_in"] = np.ones((128, 1), dtype=ml_dtypes.bfloat16)
    out["ones1_in"] = np.ones((1, 128), dtype=np.float32)
    cst = np.zeros((128, 8), dtype=np.float32)
    cst[:, 0] = 1e-5
    cst[:, 1] = 1.0
    cst[:, 2] = 0.001
    out["cst_in"] = cst
    return out


def _ffn_scl(p, layer):
    L = f"L{layer}_ffn_"
    scl = np.zeros((128, FFN_SCL_COLS), dtype=np.float32)
    scl[:, 0:32] = p[L + "w1_s"].reshape(32, 128).T
    scl[:, 32:64] = p[L + "w2_s"].reshape(32, 128).T
    scl[:, 64:96] = p[L + "h1_s"].reshape(32, 128).T
    scl[:, 96:128] = p[L + "h2_s"].reshape(32, 128).T
    scl[:, 128:136] = 0.5 * p[L + "w3_s"].reshape(8, 128).T
    return scl


def run_ffn_layer(xT, p, layer):
    nc = build_ffn_launch()
    shared = _ffn_host_inputs(p, layer)
    in_maps = [dict(shared, x=np.ascontiguousarray(xT[b])) for b in range(8)]
    res = run_bass_kernel_spmd(nc, in_maps, core_ids=list(range(8)))
    return np.stack([r["xo"] for r in res.results], axis=0)


def kernel(**inputs):
    p = inputs
    xT = np.ascontiguousarray(p["L0_attn_x"].transpose(0, 2, 1))
    xT = run_ffn_layer(xT, p, 0)
    xT = run_l0attn(xT, p)
    xT = run_ffn_layer(xT, p, 0)
    xT = run_ffn_layer(xT, p, 1)
    xT = run_l1attn(xT, p)
    xT = run_ffn_layer(xT, p, 1)
    return np.ascontiguousarray(xT.transpose(0, 2, 1))
